# version 04 (final)
# baseline (speedup 1.0000x reference)
import numpy as np
from contextlib import ExitStack

import concourse.bass as bass
import concourse.mybir as mybir
from concourse.bass_utils import run_bass_kernel_spmd

F32 = mybir.dt.float32
BF16 = mybir.dt.bfloat16
I32 = mybir.dt.int32
AF = mybir.ActivationFunctionType
ALU = mybir.AluOpType
AX = mybir.AxisListType

SEQ = 2048
D = 2048
NH = 16
NRT = 16
NCT = 16
SLOPES = [2.0 ** (-h / 2.0) for h in range(NH)]
SHIFT = 8.0
CLAMP = 80.0
LNEPS = 1e-5

ENGINES = ("sync", "tensor", "vector", "scalar", "gpsimd")
CSEM = {"tensor": "T", "vector": "V", "scalar": "SC", "gpsimd": "G", "sync": "SY"}
DSEM = {"sync": "DS", "scalar": "DC", "gpsimd": "DG"}

R_GA = 0
R_GAOB = D
R_SB = 2 * D
R_CB = 3 * D
R_WS = 4 * D


class Prog:
    def __init__(self):
        self.ops = {e: [] for e in ENGINES}
        self.cnt = {}

    def _emit(self, eng, fn, deps, sem, amt, sig):
        deps = tuple(t for t in deps if t is not None)
        tok = None
        if sig:
            v = self.cnt.get(sem, 0) + amt
            self.cnt[sem] = v
            tok = (sem, v, amt)
        self.ops[eng].append((deps, fn, tok))
        return tok

    def op(self, eng, fn, deps=(), sig=True):
        return self._emit(eng, fn, deps, CSEM[eng], 1, sig)

    def dma(self, eng, fn, deps=(), sig=True):
        return self._emit(eng, fn, deps, DSEM[eng], 16, sig)

    def barrier(self):
        return tuple((s, v, 0) for s, v in self.cnt.items())

    def realize(self, block, sems):
        for eng_name in ENGINES:
            ops = self.ops[eng_name]
            if not ops:
                continue

            def body(eng, _ops=ops):
                hw = {}
                for deps, fn, tok in _ops:
                    for dep in deps:
                        sname, val = dep[0], dep[1]
                        if hw.get(sname, 0) < val:
                            eng.wait_ge(sems[sname], val)
                            hw[sname] = val
                    ins = fn(eng)
                    if tok is not None:
                        ins.then_inc(sems[tok[0]], tok[2])

            getattr(block, eng_name)(body)


class Buf:
    def __init__(self, ap=None, seed=()):
        self.ap = ap
        self.ws = list(seed)
        self.rs = []

    def rdep(self):
        return list(self.ws)

    def wdep(self):
        return self.rs + self.ws

    def wrote(self, tok, reset=True):
        if reset:
            self.ws = []
            self.rs = []
        if tok is not None:
            self.ws.append(tok)

    def read(self, tok):
        if tok is not None:
            self.rs.append(tok)


def build():
    nc = bass.Bass("TRN2", target_bir_lowering=False)

    x_h = nc.dram_tensor("L0_attn_x", [SEQ, D], F32, kind="ExternalInput").ap()
    qkvw_h = nc.dram_tensor("L0_attn_qkv_w", [D, 3 * D], F32, kind="ExternalInput").ap()
    qkvb_h = nc.dram_tensor("L0_attn_qkv_b", [3 * D], F32, kind="ExternalInput").ap()
    outw_h = nc.dram_tensor("L0_attn_out_w", [D, D], F32, kind="ExternalInput").ap()
    outb_h = nc.dram_tensor("L0_attn_out_b", [D], F32, kind="ExternalInput").ap()
    ffnw_h = nc.dram_tensor("L0_ffn_weight", [D, D], F32, kind="ExternalInput").ap()
    c_h = nc.dram_tensor("L0_c", [D], F32, kind="ExternalInput").ap()
    wmod_h = nc.dram_tensor("L0_wmod", [D, 6 * D], F32, kind="ExternalInput").ap()
    out_h = nc.dram_tensor("out", [SEQ, D], F32, kind="ExternalOutput").ap()
    drows_h = nc.dram_tensor("scr_drows", [8 * D], F32, kind="Internal").ap()
    wqt_h = nc.dram_tensor("scr_wqt", [D, D], BF16, kind="Internal").ap()
    x2_h = nc.dram_tensor("scr_x2", [SEQ, D], F32, kind="Internal").ap()

    P = Prog()

    with ExitStack() as es:
        def mk(stack, name, shape, dtype, seed=(), **kw):
            t = stack.enter_context(nc.sbuf_tensor(name, shape, dtype, **kw))
            return Buf(t.ap() if hasattr(t, "ap") else t, seed=seed)

        def psb(name, shape):
            t = es.enter_context(nc.psum_tensor(name, shape, F32))
            return Buf(t.ap() if hasattr(t, "ap") else t)

        h1t = [mk(es, f"h1t{k}", [128, SEQ], BF16, is_transpose_dest=True)
               for k in range(NCT)]
        ctxt = [mk(es, f"ctxt{h}", [128, SEQ], BF16) for h in range(NH)]
        ones_row = mk(es, "ones_row", [1, 128], BF16)
        ones_col = mk(es, "ones_col", [128, 1], BF16)
        modg = mk(es, "modg", [128, 96], F32)
        onepca_g = mk(es, "onepca_g", [128, 16], F32)
        ga_g = mk(es, "ga_g", [128, 16], F32)
        gbw_g = mk(es, "gbw_g", [128, 16], F32)
        qb_g = mk(es, "qb_g", [128, 16], F32)
        kb_g = mk(es, "kb_g", [128, 16], F32)
        vb_g = mk(es, "vb_g", [128, 16], F32)
        wsc = mk(es, "wsc", [128, 2], F32)

        qkvps = psb("qkvps", [128, 512])
        spair = [psb(f"spair{i}", [128, 1024]) for i in range(2)]
        ctxps = psb("ctxps", [128, 1024])
        miscps = psb("miscps", [128, 512])

        scr_drows = Buf(None)
        scr_wqt = Buf(None)
        scr_x2 = Buf(None)
        scr_out = Buf(None)

        sems = {}
        for sname in list(CSEM.values()) + list(DSEM.values()):
            sems[sname] = es.enter_context(nc.semaphore(sname))

        def dma(eng, out_buf, out_ap, in_ap, deps=(), in_bufs=(), **kw):
            tok = P.dma(
                eng,
                lambda e, o=out_ap, i=in_ap, kw2=dict(kw): e.dma_start(out=o, in_=i, **kw2),
                deps=tuple(deps) + tuple(out_buf.wdep())
                + tuple(d for b in in_bufs for d in b.rdep()))
            out_buf.wrote(tok, reset=False)
            for b in in_bufs:
                b.read(tok)
            return tok

        def dmat(eng, out_buf, out_ap, in_ap, in_bufs=(), deps=()):
            tok = P.dma(
                eng,
                lambda e, o=out_ap, i=in_ap: e.dma_start_transpose(out=o, in_=i),
                deps=tuple(deps) + tuple(out_buf.wdep())
                + tuple(d for b in in_bufs for d in b.rdep()))
            out_buf.wrote(tok, reset=False)
            for b in in_bufs:
                b.read(tok)
            return tok

        def eop(engname, fn, reads, writes, deps=()):
            t = P.op(engname, fn,
                     deps=tuple(deps) + tuple(d for b in reads for d in b.rdep())
                     + tuple(d for b in writes for d in b.wdep()))
            for b in writes:
                b.wrote(t, reset=False)
            for b in reads:
                b.read(t)
            return t

        def vop(fn, reads, writes, deps=()):
            return eop("vector", fn, reads, writes, deps)

        def sop(fn, reads, writes, deps=()):
            return eop("scalar", fn, reads, writes, deps)

        def gop(fn, reads, writes, deps=()):
            return eop("gpsimd", fn, reads, writes, deps)

        def hard_barrier():
            barr = P.barrier()
            for engname in ENGINES:
                P.op(engname, lambda e: e.nop(), deps=barr, sig=True)

        gop(lambda e: e.memset(ones_row.ap[:], 1.0), [], [ones_row])
        gop(lambda e: e.memset(ones_col.ap[:], 1.0), [], [ones_col])

        with ExitStack() as esA:
            cgrid = mk(esA, "cgrid", [128, 16], F32)
            silu_c = mk(esA, "silu_c", [128, 16], F32)
            wmodb = [mk(esA, f"wmodb{i}", [128, 16 * 256], F32) for i in range(2)]
            tmpg = mk(esA, "tmpg", [128, 16], F32)
            tmpg2 = mk(esA, "tmpg2", [128, 16], F32)
            obg = mk(esA, "obg", [128, 16], F32)
            gaobg = mk(esA, "gaobg", [128, 16], F32)
            onepcb_g = mk(esA, "onepcb_g", [128, 16], F32)
            xin = [mk(esA, f"xin{i}", [128, D], F32) for i in range(2)]
            hl = [mk(esA, f"hl{i}", [128, D], BF16) for i in range(2)]
            st6 = [mk(esA, f"st6_{i}", [128, 24], F32) for i in range(2)]
            st2 = [mk(esA, f"st2_{i}", [128, 8], F32) for i in range(2)]

            dma("sync", cgrid, cgrid.ap[:], c_h[0:D].rearrange("(t p) -> p t", p=128),
                allow_slow_non_contiguous=True)
            sop(lambda e: e.activation(silu_c.ap[:], cgrid.ap[:], AF.Silu),
                [cgrid], [silu_c])
            for g, off in ((qb_g, 0), (kb_g, D), (vb_g, 2 * D)):
                dma("sync", g, g.ap[:],
                    qkvb_h[off:off + D].rearrange("(t p) -> p t", p=128),
                    allow_slow_non_contiguous=True)
            dma("sync", obg, obg.ap[:], outb_h[0:D].rearrange("(t p) -> p t", p=128),
                allow_slow_non_contiguous=True)

            for f in range(48):
                buf = wmodb[f % 2]
                eng = "sync" if f % 2 == 0 else "scalar"
                dtok = dma(eng, buf, buf.ap.rearrange("p (k c) -> p k c", c=256),
                           wmod_h[:, 256 * f:256 * (f + 1)]
                           .rearrange("(k p) c -> p k c", p=128))
                for s in range(2):
                    col = 2 * f + s
                    mt = None
                    for k in range(NCT):
                        deps = (dtok,) + tuple(silu_c.rdep())
                        if k == 0:
                            deps = deps + tuple(miscps.wdep())
                        mt = P.op("tensor",
                                  lambda e, k2=k, s2=s, c2=col, b=buf: e.matmul(
                                      miscps.ap[:, c2:c2 + 1],
                                      b.ap[:, 256 * k2 + 128 * s2:256 * k2 + 128 * (s2 + 1)],
                                      silu_c.ap[:, k2:k2 + 1],
                                      start=(k2 == 0), stop=(k2 == NCT - 1)),
                                  deps=deps, sig=(k == NCT - 1))
                    miscps.wrote(mt, reset=False)
                    silu_c.read(mt)
                    buf.read(mt)
            vop(lambda e: e.tensor_copy(modg.ap[:], miscps.ap[:, 0:96]),
                [miscps], [modg])

            def derive_gate(src_lo, dst):
                src = modg.ap[:, src_lo:src_lo + 16]
                vop(lambda e: e.tensor_scalar(tmpg.ap[:], src, 0.25, None, ALU.max),
                    [modg], [tmpg], deps=tmpg.wdep())
                vop(lambda e: e.tensor_scalar(tmpg2.ap[:], src, -0.25, None, ALU.min),
                    [modg], [tmpg2], deps=tmpg2.wdep())
                vop(lambda e: e.tensor_scalar(dst.ap[:], src, 0.0, None, ALU.is_ge),
                    [modg], [dst])
                vop(lambda e: e.tensor_tensor(tmpg.ap[:], tmpg.ap[:], tmpg2.ap[:],
                                              ALU.subtract), [tmpg2], [tmpg])
                vop(lambda e: e.tensor_tensor(tmpg.ap[:], dst.ap[:], tmpg.ap[:],
                                              ALU.mult), [dst], [tmpg])
                vop(lambda e: e.tensor_tensor(dst.ap[:], tmpg2.ap[:], tmpg.ap[:],
                                              ALU.add), [tmpg, tmpg2], [dst])

            vop(lambda e: e.tensor_scalar(onepca_g.ap[:], modg.ap[:, 16:32],
                                          1.0, None, ALU.add), [modg], [onepca_g])
            vop(lambda e: e.tensor_scalar(onepcb_g.ap[:], modg.ap[:, 64:80],
                                          1.0, None, ALU.add), [modg], [onepcb_g])
            derive_gate(32, ga_g)
            derive_gate(80, gbw_g)
            vop(lambda e: e.tensor_tensor(gaobg.ap[:], ga_g.ap[:], obg.ap[:],
                                          ALU.mult), [ga_g, obg], [gaobg])
            dma("sync", scr_drows, drows_h[R_GA:R_GA + D].rearrange("(t p) -> p t", p=128),
                ga_g.ap[:], in_bufs=(ga_g,), allow_slow_non_contiguous=True)
            dma("sync", scr_drows, drows_h[R_GAOB:R_GAOB + D].rearrange("(t p) -> p t", p=128),
                gaobg.ap[:], in_bufs=(gaobg,), allow_slow_non_contiguous=True)
            dma("sync", scr_drows, drows_h[R_SB:R_SB + D].rearrange("(t p) -> p t", p=128),
                modg.ap[:, 48:64], in_bufs=(modg,), allow_slow_non_contiguous=True)
            dma("sync", scr_drows, drows_h[R_CB:R_CB + D].rearrange("(t p) -> p t", p=128),
                onepcb_g.ap[:], in_bufs=(onepcb_g,), allow_slow_non_contiguous=True)

            for r in range(NRT):
                xb, hb, s6, s2 = xin[r % 2], hl[r % 2], st6[r % 2], st2[r % 2]
                dtok = dma("sync", xb, xb.ap[:], x_h[128 * r:128 * (r + 1), :])
                wd = tuple(s6.wdep())
                for c in range(4):
                    t4 = vop(lambda e, b=xb, o=s6, c2=c: e.bn_stats(
                        o.ap[:, 6 * c2:6 * c2 + 6], b.ap[:, 512 * c2:512 * c2 + 512]),
                        [xb], [s6], deps=(dtok,) + (wd if c == 0 else ()))
                s6.ws, s6.rs = list(s6.ws[-4:]), []
                vop(lambda e, i=s6, o=s2: e.bn_aggr(o.ap[:, 0:2], i.ap[:]), [s6], [s2],
                    deps=s2.wdep())
                s2.ws, s2.rs = [s2.ws[-1]], []
                vop(lambda e, o=s2: e.tensor_scalar(o.ap[:, 2:3], o.ap[:, 1:2],
                                                    LNEPS, None, ALU.add), [s2], [s2])
                sop(lambda e, o=s2: e.activation(o.ap[:, 2:3], o.ap[:, 2:3], AF.Sqrt),
                    [s2], [s2])
                vop(lambda e, o=s2: e.reciprocal(o.ap[:, 3:4], o.ap[:, 2:3]), [s2], [s2])
                vop(lambda e, o=s2: e.scalar_tensor_tensor(o.ap[:, 4:5], o.ap[:, 0:1],
                                                           -1.0, o.ap[:, 3:4],
                                                           ALU.mult, ALU.mult),
                    [s2], [s2])
                sop(lambda e, b=xb, o=hb, st=s2: e.activation(
                    o.ap[:], b.ap[:], AF.Identity, bias=st.ap[:, 4:5],
                    scale=st.ap[:, 3:4]), [xb, s2], [hb], deps=hb.wdep())
                hb.ws, hb.rs = [hb.ws[-1]], []
                for k in range(NCT):
                    eng = "sync" if (k % 2 == 0) else "scalar"
                    dmat(eng, h1t[k], h1t[k].ap[:, 128 * r:128 * (r + 1)],
                         hb.ap[:, 128 * k:128 * (k + 1)], in_bufs=(hb,))

            for k in range(NCT):
                sop(lambda e, k2=k: e.activation(h1t[k2].ap[:], h1t[k2].ap[:],
                                                 AF.Identity,
                                                 bias=modg.ap[:, k2:k2 + 1],
                                                 scale=onepca_g.ap[:, k2:k2 + 1]),
                    [h1t[k], modg, onepca_g], [h1t[k]])

        bar1 = P.barrier()
        with ExitStack() as esB:
            masks = [mk(esB, f"mask{d}", [128, 512], BF16, seed=bar1) for d in range(4)]
            jgrid = mk(esB, "jgrid", [128, 16], F32, seed=bar1)
            bias_g = [mk(esB, f"biasg{h}", [128, 16], F32, seed=bar1) for h in range(NH)]
            iota_row = mk(esB, "iota_row", [1, SEQ], BF16, seed=bar1)
            grow = mk(esB, "grow", [1, SEQ], BF16, seed=bar1)
            wqkv = mk(esB, "wqkv", [128, 16 * 384], BF16, seed=bar1)
            qstg = mk(esB, "qstg", [128, 16 * 128], F32, seed=bar1)
            qT = mk(esB, "qT", [128, SEQ], BF16, seed=bar1)
            kT = mk(esB, "kT", [128, SEQ], BF16, seed=bar1)
            vT = mk(esB, "vT", [128, SEQ], BF16, seed=bar1)
            vN = mk(esB, "vN", [128, SEQ], BF16, seed=bar1, is_transpose_dest=True)
            pbuf = [mk(esB, f"pbuf{i}", [128, 1024], BF16, seed=bar1) for i in range(2)]
            sclamp = mk(esB, "sclamp", [128, 512], F32, seed=bar1)
            dnacc = mk(esB, "dnacc", [128, 1024], F32, seed=bar1)
            dnbf = mk(esB, "dnbf", [128, 1024], BF16, seed=bar1)
            recf = mk(esB, "recf", [1, 512], F32, seed=bar1)
            recbf = mk(esB, "recbf", [1, 512], BF16, seed=bar1)
            ctmp = mk(esB, "ctmp", [128, 512], F32, seed=bar1)
            rbc = mk(esB, "rbc", [128, 512], F32, seed=bar1)

            gop(lambda e: e.iota(jgrid.ap[:], [[128, 16]], channel_multiplier=1,
                                 allow_small_or_imprecise_dtypes=True), [], [jgrid])
            gop(lambda e: e.iota(iota_row.ap[:], [[1, SEQ]], channel_multiplier=0,
                                 allow_small_or_imprecise_dtypes=True), [], [iota_row])
            for d in range(4):
                gop(lambda e, m=masks[d]: e.memset(m.ap[:], 1.0), [], [masks[d]])
                gop(lambda e, m=masks[d], dd=d: e.affine_select(
                    out=m.ap[:], in_=m.ap[:], compare_op=ALU.is_ge, fill=0.0,
                    base=-128 * dd, pattern=[[1, 512]], channel_multiplier=-1),
                    [], [masks[d]])
            for h in range(NH):
                vop(lambda e, h2=h: e.tensor_scalar(bias_g[h2].ap[:], jgrid.ap[:],
                                                    float(SLOPES[h2]), None, ALU.mult),
                    [jgrid], [bias_g[h]])

            def load_head_w(h):
                v3 = wqkv.ap.rearrange("p (k w) -> p k w", w=384)
                s3 = qstg.ap.rearrange("p (k w) -> p k w", w=128)
                toks = []
                for p2, off in ((0, 0), (1, D), (2, 2 * D)):
                    eng = ("sync", "scalar", "sync")[p2]
                    ltok = dma(eng, qstg, s3,
                               qkvw_h[:, off + 128 * h:off + 128 * (h + 1)]
                               .rearrange("(k p) w -> p k w", p=128),
                               deps=qstg.wdep())
                    qstg.ws, qstg.rs = [ltok], []
                    ceng = "scalar" if p2 == 1 else "vector"
                    toks.append(eop(ceng,
                                    lambda e, p3=p2: (e.copy if p3 == 1 else e.tensor_copy)(
                                        v3[:, :, 128 * p3:128 * (p3 + 1)], s3),
                                    [qstg], [wqkv]))
                return toks

            def emit_qkv_block(h, proj, nb, wtoks):
                dst = (qT, kT, vT)[proj]
                mt = None
                for k in range(NCT):
                    deps = tuple(wtoks) + tuple(h1t[k].rdep())
                    if k == 0:
                        deps = deps + tuple(qkvps.wdep())
                    mt = P.op("tensor",
                              lambda e, k2=k, p2=proj: e.matmul(
                                  qkvps.ap[:],
                                  wqkv.ap[:, 384 * k2 + 128 * p2:384 * k2 + 128 * (p2 + 1)],
                                  h1t[k2].ap[:, 512 * nb:512 * nb + 512],
                                  start=(k2 == 0), stop=(k2 == NCT - 1)),
                              deps=deps, sig=(k == NCT - 1))
                qkvps.wrote(mt)
                wqkv.read(mt)
                for k in range(NCT):
                    h1t[k].read(mt)
                if proj == 2:
                    vop(lambda e, d2=dst, nb2=nb: e.tensor_copy(
                        d2.ap[:, 512 * nb2:512 * nb2 + 512], qkvps.ap[:]),
                        [qkvps], [dst])
                else:
                    bg = qb_g if proj == 0 else kb_g
                    sop(lambda e, d2=dst, nb2=nb, h2=h, b=bg: e.activation(
                        d2.ap[:, 512 * nb2:512 * nb2 + 512], qkvps.ap[:], AF.Identity,
                        bias=b.ap[:, h2:h2 + 1]), [qkvps, bg], [dst])

            def emit_attention(h):
                pb_i = 0
                for qb2 in range(2):
                    q0 = 1024 * qb2
                    nkt = 8 * (qb2 + 1)
                    ctx_started = [False, False]
                    dn_started = [False, False]
                    for kt in range(nkt):
                        halves = [j for j in range(2) if kt <= 8 * qb2 + 4 * j + 3]
                        spb = spair[pb_i % 2]
                        pb = pbuf[pb_i % 2]
                        pb_i += 1
                        mt = None
                        for ji, j in enumerate(halves):
                            deps = tuple(kT.rdep()) + tuple(qT.rdep())
                            if ji == 0:
                                deps = deps + tuple(spb.wdep())
                            mt = P.op("tensor",
                                      lambda e, j2=j, kt2=kt, s3=spb, q2=q0: e.matmul(
                                          s3.ap[:, 512 * j2:512 * j2 + 512],
                                          kT.ap[:, 128 * kt2:128 * (kt2 + 1)],
                                          qT.ap[:, q2 + 512 * j2:q2 + 512 * j2 + 512],
                                          start=True, stop=False),
                                      deps=deps, sig=False)
                        for ji, j in enumerate(halves):
                            mt = P.op("tensor",
                                      lambda e, j2=j, s3=spb, q2=q0: e.matmul(
                                          s3.ap[:, 512 * j2:512 * j2 + 512],
                                          ones_row.ap[:],
                                          grow.ap[0:1, q2 + 512 * j2:q2 + 512 * j2 + 512],
                                          start=False, stop=True),
                                      deps=tuple(ones_row.rdep()) + tuple(grow.rdep()),
                                      sig=(ji == len(halves) - 1))
                        spb.wrote(mt)
                        for b in (kT, qT, ones_row, grow):
                            b.read(mt)
                        first_pb = True
                        for j in halves:
                            dlt = kt - (8 * qb2 + 4 * j)
                            lo = 512 * j
                            if dlt < 0:
                                et = sop(lambda e, lo2=lo, kt2=kt, h2=h, s3=spb, p3=pb:
                                         e.activation(p3.ap[:, lo2:lo2 + 512],
                                                      s3.ap[:, lo2:lo2 + 512], AF.Exp,
                                                      bias=bias_g[h2].ap[:, kt2:kt2 + 1]),
                                         [spb, bias_g[h]], [pb],
                                         deps=pb.wdep() if first_pb else ())
                            else:
                                ct = vop(lambda e, lo2=lo, kt2=kt, h2=h, s3=spb:
                                         e.tensor_scalar(sclamp.ap[:],
                                                         s3.ap[:, lo2:lo2 + 512],
                                                         bias_g[h2].ap[:, kt2:kt2 + 1],
                                                         CLAMP, ALU.add, ALU.min),
                                         [spb, bias_g[h]], [sclamp],
                                         deps=sclamp.wdep())
                                sclamp.ws, sclamp.rs = [ct], []
                                et = sop(lambda e, lo2=lo, p3=pb: e.activation(
                                    p3.ap[:, lo2:lo2 + 512], sclamp.ap[:], AF.Exp),
                                    [sclamp], [pb],
                                    deps=pb.wdep() if first_pb else ())
                            if first_pb:
                                pb.ws, pb.rs = [et], []
                                first_pb = False
                            if dlt >= 0:
                                vop(lambda e, lo2=lo, m=masks[dlt], p3=pb: e.tensor_tensor(
                                    p3.ap[:, lo2:lo2 + 512], p3.ap[:, lo2:lo2 + 512],
                                    m.ap[:], ALU.mult), [masks[dlt]], [pb])
                        for j in halves:
                            if not dn_started[j]:
                                dn_started[j] = True
                                vop(lambda e, j2=j, p3=pb: e.tensor_copy(
                                    dnacc.ap[:, 512 * j2:512 * j2 + 512],
                                    p3.ap[:, 512 * j2:512 * j2 + 512]), [pb], [dnacc])
                            else:
                                vop(lambda e, j2=j, p3=pb: e.tensor_tensor(
                                    dnacc.ap[:, 512 * j2:512 * j2 + 512],
                                    dnacc.ap[:, 512 * j2:512 * j2 + 512],
                                    p3.ap[:, 512 * j2:512 * j2 + 512], ALU.add),
                                    [pb], [dnacc])
                        mt = None
                        for ji, j in enumerate(halves):
                            first = not ctx_started[j]
                            ctx_started[j] = True
                            deps = tuple(vN.rdep()) + tuple(pb.rdep())
                            if first and kt == 0:
                                deps = deps + tuple(ctxps.wdep())
                            mt = P.op("tensor",
                                      lambda e, j2=j, kt2=kt, f=first, p3=pb, qb3=qb2:
                                      e.matmul(
                                          ctxps.ap[:, 512 * j2:512 * j2 + 512],
                                          vN.ap[:, 128 * kt2:128 * (kt2 + 1)],
                                          p3.ap[:, 512 * j2:512 * j2 + 512],
                                          start=f, stop=(kt2 == 8 * qb3 + 4 * j2 + 3)),
                                      deps=deps, sig=(ji == len(halves) - 1))
                        if kt == 0:
                            ctxps.wrote(mt)
                        else:
                            ctxps.wrote(mt, reset=False)
                        vN.read(mt)
                        pb.read(mt)
                    t3 = vop(lambda e: e.tensor_copy(dnbf.ap[:], dnacc.ap[:]),
                             [dnacc], [dnbf], deps=dnbf.wdep())
                    dnbf.ws, dnbf.rs = [t3], []
                    for j in range(2):
                        deps = tuple(miscps.wdep()) + tuple(dnbf.rdep()) \
                            + tuple(ones_col.rdep())
                        mt = P.op("tensor",
                                  lambda e, j2=j: e.matmul(
                                      miscps.ap[0:1, 0:512], ones_col.ap[:],
                                      dnbf.ap[:, 512 * j2:512 * j2 + 512],
                                      start=True, stop=True), deps=deps)
                        miscps.wrote(mt)
                        dnbf.read(mt)
                        ones_col.read(mt)
                        t3 = vop(lambda e: e.reciprocal(recf.ap[:], miscps.ap[0:1, 0:512]),
                                 [miscps], [recf], deps=recf.wdep())
                        recf.ws, recf.rs = [t3], []
                        t3 = sop(lambda e: e.copy(recbf.ap[:], recf.ap[:]),
                                 [recf], [recbf], deps=recbf.wdep())
                        recbf.ws, recbf.rs = [t3], []
                        deps = tuple(miscps.wdep()) + tuple(recbf.rdep()) \
                            + tuple(ones_row.rdep())
                        mt = P.op("tensor",
                                  lambda e: e.matmul(
                                      miscps.ap[:, 0:512], ones_row.ap[:],
                                      recbf.ap[0:1, 0:512], start=True, stop=True),
                                  deps=deps)
                        miscps.wrote(mt)
                        recbf.read(mt)
                        ones_row.read(mt)
                        t3 = vop(lambda e: e.tensor_copy(rbc.ap[:], miscps.ap[:, 0:512]),
                                 [miscps], [rbc], deps=rbc.wdep())
                        rbc.ws, rbc.rs = [t3], []
                        t3 = vop(lambda e, j2=j: e.tensor_tensor(
                            ctmp.ap[:], ctxps.ap[:, 512 * j2:512 * j2 + 512],
                            rbc.ap[:], ALU.mult), [ctxps, rbc], [ctmp],
                            deps=ctmp.wdep())
                        ctmp.ws, ctmp.rs = [t3], []
                        sop(lambda e, j2=j, h2=h, q2=q0: e.activation(
                            ctxt[h2].ap[:, q2 + 512 * j2:q2 + 512 * j2 + 512],
                            ctmp.ap[:], AF.Identity,
                            bias=vb_g.ap[:, h2:h2 + 1]), [ctmp, vb_g], [ctxt[h]])

            wtoks_cur = load_head_w(0)
            for h in range(NH):
                vop(lambda e, h2=h: e.tensor_scalar(grow.ap[:], iota_row.ap[:],
                                                    float(-SLOPES[h2]), -SHIFT,
                                                    ALU.mult, ALU.add),
                    [iota_row], [grow], deps=grow.wdep())
                grow.ws, grow.rs = [grow.ws[-1]], []
                for proj in range(3):
                    for nb in range(4):
                        emit_qkv_block(h, proj, nb, wtoks_cur)
                for k in range(NCT):
                    eng = "sync" if (k % 2 == 0) else "scalar"
                    dmat(eng, vN, vN.ap[:, 128 * k:128 * (k + 1)],
                         vT.ap[:, 128 * k:128 * (k + 1)], in_bufs=(vT,))
                if h + 1 < NH:
                    wtoks_cur = load_head_w(h + 1)
                emit_attention(h)
                hard_barrier()

        bar2 = P.barrier()
        with ExitStack() as esC:
            wfin = mk(esC, "wfin", [128, D], F32, seed=bar2)
            wacc = mk(esC, "wacc", [128, 2], F32, seed=bar2)
            whl = mk(esC, "whl", [128, 2], BF16, seed=bar2)
            wsrow = mk(esC, "wsrow", [1, 8], F32, seed=bar2)
            wqf = mk(esC, "wqf", [128, 512], F32, seed=bar2)
            wqi = mk(esC, "wqi", [128, 512], I32, seed=bar2)
            wqn = mk(esC, "wqn", [128, D], BF16, seed=bar2)
            wqtt = mk(esC, "wqtt", [128, D], BF16, seed=bar2, is_transpose_dest=True)
            woutb = [mk(esC, f"woutb{i}", [128, 16 * 512], BF16, seed=bar2)
                     for i in range(1)]
            wstg = mk(esC, "wstg", [128, 8 * 512], F32, seed=bar2)
            gab = mk(esC, "gab", [128, 512], F32, seed=bar2)
            xst = [mk(esC, f"xst{i}", [128, 512], F32, seed=bar2) for i in range(2)]
            x2st = [mk(esC, f"x2st{i}", [128, 512], F32, seed=bar2) for i in range(2)]
            rowf = mk(esC, "rowf", [1, D], F32, seed=bar2)
            gaob_bf = mk(esC, "gaob_bf", [1, D], BF16, seed=bar2)

            for r in range(NRT):
                dtok = dma("sync", wfin, wfin.ap[:], ffnw_h[128 * r:128 * (r + 1), :])
                col = 0 if r == 0 else 1
                vop(lambda e, c2=col: e.tensor_reduce(wacc.ap[:, c2:c2 + 1], wfin.ap[:],
                                                      AX.X, ALU.add,
                                                      apply_absolute_value=True),
                    [wfin], [wacc], deps=(dtok,))
                if r > 0:
                    vop(lambda e: e.tensor_tensor(wacc.ap[:, 0:1], wacc.ap[:, 0:1],
                                                  wacc.ap[:, 1:2], ALU.add),
                        [], [wacc])
            vop(lambda e: e.tensor_copy(whl.ap[:, 0:1], wacc.ap[:, 0:1]), [wacc], [whl])
            vop(lambda e: e.tensor_tensor(wacc.ap[:, 1:2], wacc.ap[:, 0:1],
                                          whl.ap[:, 0:1], ALU.subtract),
                [whl], [wacc])
            vop(lambda e: e.tensor_copy(whl.ap[:, 1:2], wacc.ap[:, 1:2]), [wacc], [whl])
            mt = None
            for j in range(2):
                deps = tuple(whl.rdep()) + tuple(ones_col.rdep()) + tuple(miscps.wdep())
                mt = P.op("tensor",
                          lambda e, j2=j: e.matmul(miscps.ap[0:1, 0:1],
                                                   whl.ap[:, j2:j2 + 1], ones_col.ap[:],
                                                   start=(j2 == 0), stop=(j2 == 1)),
                          deps=deps, sig=(j == 1))
            miscps.wrote(mt)
            whl.read(mt)
            ones_col.read(mt)
            vop(lambda e: e.tensor_scalar(wsrow.ap[0:1, 0:1], miscps.ap[0:1, 0:1],
                                          1.0 / float(D * D), 1e-6, ALU.mult, ALU.max),
                [miscps], [wsrow])
            vop(lambda e: e.reciprocal(wsrow.ap[0:1, 1:2], wsrow.ap[0:1, 0:1]),
                [], [wsrow])
            vop(lambda e: e.tensor_scalar(wsrow.ap[0:1, 2:3], wsrow.ap[0:1, 0:1],
                                          1.0 / 127.0, None, ALU.mult), [], [wsrow])
            t = dma("sync", scr_drows, drows_h[R_WS:R_WS + 8], wsrow.ap[0:1, 0:8],
                    in_bufs=(wsrow,))
            t1 = dma("sync", wsc, wsc.ap[:, 0:1],
                     drows_h[R_WS + 1:R_WS + 2].partition_broadcast(128), deps=(t,))
            t2 = dma("sync", wsc, wsc.ap[:, 1:2],
                     drows_h[R_WS + 2:R_WS + 3].partition_broadcast(128), deps=(t,))
            scr_drows.read(t1)
            scr_drows.read(t2)

            def emit_wquant_rowtile(r):
                dtok = dma("sync", wfin, wfin.ap[:], ffnw_h[128 * r:128 * (r + 1), :])
                for c in range(4):
                    lo = 512 * c
                    t4 = vop(lambda e, lo2=lo: e.tensor_scalar(
                        wqf.ap[:], wfin.ap[:, lo2:lo2 + 512], wsc.ap[:, 0:1], None,
                        ALU.mult), [wfin, wsc], [wqf], deps=(dtok,) + tuple(wqf.wdep()))
                    wqf.ws, wqf.rs = [t4], []
                    t4 = vop(lambda e: e.tensor_copy(wqi.ap[:], wqf.ap[:]), [wqf], [wqi],
                             deps=wqi.wdep())
                    wqi.ws, wqi.rs = [t4], []
                    vop(lambda e, lo2=lo: e.tensor_copy(wqn.ap[:, lo2:lo2 + 512],
                                                        wqi.ap[:]), [wqi], [wqn])
                vop(lambda e: e.tensor_scalar(wqn.ap[:], wqn.ap[:], -1.0, 1.0,
                                              ALU.max, ALU.min), [], [wqn])
                t4 = vop(lambda e, r2=r: e.tensor_scalar(wqn.ap[:], wqn.ap[:],
                                                         gbw_g.ap[:, r2:r2 + 1], None,
                                                         ALU.mult), [gbw_g], [wqn])
                wqn.ws, wqn.rs = [t4], []
                for k in range(NCT):
                    eng = "sync" if (k % 2 == 0) else "scalar"
                    ttok = dmat(eng, wqtt, wqtt.ap[:, 128 * k:128 * (k + 1)],
                                wqn.ap[:, 128 * k:128 * (k + 1)], in_bufs=(wqn,))
                    stok = dma(eng, scr_wqt,
                               wqt_h[128 * k:128 * (k + 1), 128 * r:128 * (r + 1)],
                               wqtt.ap[:, 128 * k:128 * (k + 1)], deps=(ttok,))
                    wqtt.read(stok)

            t = dma("scalar", rowf, rowf.ap[:], drows_h[R_GAOB:R_GAOB + D],
                    deps=tuple(scr_drows.rdep()))
            scr_drows.read(t)
            sop(lambda e: e.copy(gaob_bf.ap[:], rowf.ap[:]),
                [rowf], [gaob_bf])

            for nb in range(4):
                wob = woutb[0]
                v3 = wob.ap.rearrange("p (k w) -> p k w", w=512)
                sg3 = wstg.ap.rearrange("p (k w) -> p k w", w=512)
                wtok = None
                for hh in range(2):
                    ltok = dma("sync" if hh == 0 else "scalar", wstg, sg3,
                               outw_h[1024 * hh:1024 * (hh + 1), 512 * nb:512 * nb + 512]
                               .rearrange("(k p) w -> p k w", p=128),
                               deps=wstg.wdep())
                    wstg.ws, wstg.rs = [ltok], []
                    wtok = eop("scalar" if hh == 0 else "vector",
                               lambda e, h3=hh: (e.copy if h3 == 0 else e.tensor_copy)(
                                   v3[:, 8 * h3:8 * (h3 + 1), :], sg3),
                               [wstg], [wob])
                gtok = dma("scalar", gab, gab.ap[:],
                           drows_h[R_GA + 512 * nb:R_GA + 512 * (nb + 1)]
                           .partition_broadcast(128),
                           deps=tuple(scr_drows.rdep()) + tuple(gab.wdep()))
                gab.ws, gab.rs = [gtok], []
                scr_drows.read(gtok)
                for k in range(NCT):
                    vop(lambda e, k2=k, w=wob: e.tensor_tensor(
                        w.ap[:, 512 * k2:512 * k2 + 512],
                        w.ap[:, 512 * k2:512 * k2 + 512],
                        gab.ap[:], ALU.mult), [gab], [wob], deps=(wtok, gtok))
                for r in range(NRT):
                    spb = spair[r % 2]
                    mt = None
                    for k in range(NCT):
                        deps = tuple(wob.rdep()) + tuple(ctxt[k].rdep())
                        if k == 0:
                            deps = deps + tuple(spb.wdep())
                        mt = P.op("tensor",
                                  lambda e, k2=k, r2=r, w=wob, s3=spb: e.matmul(
                                      s3.ap[:, 0:512],
                                      ctxt[k2].ap[:, 128 * r2:128 * (r2 + 1)],
                                      w.ap[:, 512 * k2:512 * k2 + 512],
                                      start=(k2 == 0), stop=False),
                                  deps=deps, sig=False)
                    mt = P.op("tensor",
                              lambda e, nb2=nb, s3=spb: e.matmul(
                                  s3.ap[:, 0:512], ones_row.ap[:],
                                  gaob_bf.ap[0:1, 512 * nb2:512 * nb2 + 512],
                                  start=False, stop=True),
                              deps=tuple(ones_row.rdep()) + tuple(gaob_bf.rdep()),
                              sig=True)
                    spb.wrote(mt)
                    wob.read(mt)
                    for k in range(NCT):
                        ctxt[k].read(mt)
                    ones_row.read(mt)
                    gaob_bf.read(mt)
                    xb, x2b = xst[r % 2], x2st[r % 2]
                    dtok = dma("scalar", xb, xb.ap[:],
                               x_h[128 * r:128 * (r + 1), 512 * nb:512 * nb + 512])
                    vop(lambda e, s3=spb, a=xb, o=x2b: e.tensor_tensor(
                        o.ap[:], s3.ap[:, 0:512], a.ap[:], ALU.add),
                        [spb, xb], [x2b], deps=(dtok,) + tuple(x2b.wdep()))
                    dma("sync", scr_x2,
                        x2_h[128 * r:128 * (r + 1), 512 * nb:512 * nb + 512],
                        x2b.ap[:], in_bufs=(x2b,))
                for r in range(4 * nb, 4 * nb + 4):
                    emit_wquant_rowtile(r)
                hard_barrier()

        bar3 = P.barrier()
        with ExitStack() as esD:
            x2t = [mk(esD, f"x2t{i}", [128, D], F32, seed=bar3) for i in range(2)]
            h2t = mk(esD, "h2t", [128, D], F32, seed=bar3)
            wqfB = mk(esD, "wqfB", [128, 512], F32, seed=bar3)
            xqiB = mk(esD, "xqiB", [128, 512], I32, seed=bar3)
            xqn = mk(esD, "xqn", [128, D], BF16, seed=bar3)
            xqT = mk(esD, "xqT", [128, D], BF16, seed=bar3, is_transpose_dest=True)
            outt = [mk(esD, f"outt{i}", [128, 512], F32, seed=bar3) for i in range(2)]
            sb_b = mk(esD, "sb_b", [128, D], F32, seed=bar3)
            onepcb_b = mk(esD, "onepcb_b", [128, D], F32, seed=bar3)
            st6 = [mk(esD, f"st6D{i}", [128, 24], F32, seed=bar3) for i in range(2)]
            st2 = [mk(esD, f"st2D{i}", [128, 8], F32, seed=bar3) for i in range(2)]

            t1 = dma("scalar", sb_b, sb_b.ap[:],
                     drows_h[R_SB:R_SB + D].partition_broadcast(128),
                     deps=tuple(scr_drows.rdep()))
            t2 = dma("scalar", onepcb_b, onepcb_b.ap[:],
                     drows_h[R_CB:R_CB + D].partition_broadcast(128),
                     deps=tuple(scr_drows.rdep()))
            scr_drows.read(t1)
            scr_drows.read(t2)

            wqt_toks = []
            for k in range(NCT):
                eng = "sync" if (k % 2 == 0) else "scalar"
                t = dma(eng, h1t[k], h1t[k].ap[:], wqt_h[128 * k:128 * (k + 1), :],
                        deps=tuple(scr_wqt.rdep()))
                scr_wqt.read(t)
                wqt_toks.append(t)

            out_toks = []
            for r in range(NRT):
                x2b, s6, s2 = x2t[r % 2], st6[r % 2], st2[r % 2]
                dtok = dma("sync", x2b, x2b.ap[:], x2_h[128 * r:128 * (r + 1), :],
                           deps=tuple(scr_x2.rdep()))
                scr_x2.read(dtok)
                wd = tuple(s6.wdep())
                for c in range(4):
                    t4 = vop(lambda e, b=x2b, o=s6, c2=c: e.bn_stats(
                        o.ap[:, 6 * c2:6 * c2 + 6], b.ap[:, 512 * c2:512 * c2 + 512]),
                        [x2b], [s6], deps=(dtok,) + (wd if c == 0 else ()))
                s6.ws, s6.rs = list(s6.ws[-4:]), []
                vop(lambda e, i=s6, o=s2: e.bn_aggr(o.ap[:, 0:2], i.ap[:]), [s6], [s2],
                    deps=s2.wdep())
                s2.ws, s2.rs = [s2.ws[-1]], []
                vop(lambda e, o=s2: e.tensor_scalar(o.ap[:, 2:3], o.ap[:, 1:2],
                                                    LNEPS, None, ALU.add), [s2], [s2])
                sop(lambda e, o=s2: e.activation(o.ap[:, 2:3], o.ap[:, 2:3], AF.Sqrt),
                    [s2], [s2])
                vop(lambda e, o=s2: e.reciprocal(o.ap[:, 3:4], o.ap[:, 2:3]), [s2], [s2])
                vop(lambda e, o=s2: e.scalar_tensor_tensor(o.ap[:, 4:5], o.ap[:, 0:1],
                                                           -1.0, o.ap[:, 3:4],
                                                           ALU.mult, ALU.mult),
                    [s2], [s2])
                t4 = sop(lambda e, b=x2b, st=s2: e.activation(
                    h2t.ap[:], b.ap[:], AF.Identity, bias=st.ap[:, 4:5],
                    scale=st.ap[:, 3:4]), [x2b, s2], [h2t], deps=h2t.wdep())
                h2t.ws, h2t.rs = [t4], []
                vop(lambda e: e.tensor_tensor(h2t.ap[:], h2t.ap[:], onepcb_b.ap[:],
                                              ALU.mult), [onepcb_b], [h2t])
                vop(lambda e: e.tensor_tensor(h2t.ap[:], h2t.ap[:], sb_b.ap[:],
                                              ALU.add), [sb_b], [h2t])
                vop(lambda e, o=s2: e.tensor_reduce(o.ap[:, 5:6], h2t.ap[:], AX.X,
                                                    ALU.max, apply_absolute_value=True),
                    [h2t], [s2])
                vop(lambda e, o=s2: e.tensor_scalar(o.ap[:, 5:6], o.ap[:, 5:6], 1e-8,
                                                    None, ALU.max), [], [s2])
                vop(lambda e, o=s2: e.reciprocal(o.ap[:, 6:7], o.ap[:, 5:6]), [s2], [s2])
                vop(lambda e, o=s2: e.tensor_scalar(o.ap[:, 6:7], o.ap[:, 6:7], 127.0,
                                                    None, ALU.mult), [], [s2])
                vop(lambda e, o=s2: e.tensor_tensor(o.ap[:, 7:8], o.ap[:, 5:6],
                                                    wsc.ap[:, 1:2], ALU.mult),
                    [wsc], [s2])
                for c in range(4):
                    lo = 512 * c
                    t4 = vop(lambda e, lo2=lo, o=s2: e.tensor_scalar(
                        wqfB.ap[:], h2t.ap[:, lo2:lo2 + 512], o.ap[:, 6:7], None,
                        ALU.mult), [h2t, s2], [wqfB], deps=wqfB.wdep())
                    wqfB.ws, wqfB.rs = [t4], []
                    t4 = vop(lambda e: e.tensor_copy(xqiB.ap[:], wqfB.ap[:]),
                             [wqfB], [xqiB], deps=xqiB.wdep())
                    xqiB.ws, xqiB.rs = [t4], []
                    vop(lambda e, lo2=lo: e.tensor_copy(xqn.ap[:, lo2:lo2 + 512],
                                                        xqiB.ap[:]), [xqiB], [xqn])
                for k in range(NCT):
                    eng = "sync" if (k % 2 == 0) else "scalar"
                    dmat(eng, xqT, xqT.ap[:, 128 * k:128 * (k + 1)],
                         xqn.ap[:, 128 * k:128 * (k + 1)], in_bufs=(xqn,))
                for nb in range(4):
                    spb = spair[nb % 2]
                    ob = outt[nb % 2]
                    mt = None
                    for k in range(NCT):
                        deps = tuple(xqT.rdep()) + (wqt_toks[k],) + tuple(h1t[k].rdep())
                        if k == 0:
                            deps = deps + tuple(spb.wdep())
                        mt = P.op("tensor",
                                  lambda e, k2=k, nb2=nb, s3=spb: e.matmul(
                                      s3.ap[:, 0:512],
                                      xqT.ap[:, 128 * k2:128 * (k2 + 1)],
                                      h1t[k2].ap[:, 512 * nb2:512 * nb2 + 512],
                                      start=(k2 == 0), stop=(k2 == NCT - 1)),
                                  deps=deps, sig=(k == NCT - 1))
                    spb.wrote(mt)
                    xqT.read(mt)
                    for k in range(NCT):
                        h1t[k].read(mt)
                    vop(lambda e, s3=spb, nb2=nb, o=ob, xb=x2b, sc=s2:
                        e.scalar_tensor_tensor(
                            o.ap[:], s3.ap[:, 0:512], sc.ap[:, 7:8],
                            xb.ap[:, 512 * nb2:512 * nb2 + 512], ALU.mult, ALU.add),
                        [spb, s2, x2b], [ob], deps=ob.wdep())
                    ob.ws, ob.rs = [ob.ws[-1]], []
                    eng = "sync" if (nb % 2 == 0) else "scalar"
                    ot = dma(eng, scr_out,
                             out_h[128 * r:128 * (r + 1), 512 * nb:512 * nb + 512],
                             ob.ap[:], in_bufs=(ob,))
                    out_toks.append(ot)
                hard_barrier()

        P.op("sync", lambda e: e.nop(), deps=tuple(out_toks), sig=False)

        block = es.enter_context(nc.Block())
        P.realize(block, sems)

    return nc


_NC = None


def kernel(L0_attn_x, L0_attn_qkv_w, L0_attn_qkv_b, L0_attn_out_w, L0_attn_out_b,
           L0_ffn_weight, L0_c, L0_wmod):
    global _NC
    if _NC is None:
        _NC = build()
    x = np.ascontiguousarray(L0_attn_x, dtype=np.float32)
    qkvw = np.ascontiguousarray(L0_attn_qkv_w, dtype=np.float32)
    qkvb = np.ascontiguousarray(L0_attn_qkv_b, dtype=np.float32)
    outw = np.ascontiguousarray(L0_attn_out_w, dtype=np.float32)
    outb = np.ascontiguousarray(L0_attn_out_b, dtype=np.float32)
    ffnw = np.ascontiguousarray(L0_ffn_weight, dtype=np.float32)
    c = np.ascontiguousarray(L0_c, dtype=np.float32)
    wmod = np.ascontiguousarray(L0_wmod, dtype=np.float32)
    in_maps = []
    for b in range(8):
        in_maps.append({
            "L0_attn_x": x[b],
            "L0_attn_qkv_w": qkvw,
            "L0_attn_qkv_b": qkvb,
            "L0_attn_out_w": outw,
            "L0_attn_out_b": outb,
            "L0_ffn_weight": ffnw,
            "L0_c": np.ascontiguousarray(c[b]),
            "L0_wmod": wmod,
        })
    res = run_bass_kernel_spmd(_NC, in_maps, core_ids=list(range(8)))
    return np.stack([res.results[b]["out"] for b in range(8)], axis=0)
